# version 03
# baseline (speedup 1.0000x reference)
import numpy as np
import ml_dtypes
import concourse.bass as bass
import concourse.mybir as mybir
from concourse.bass_utils import run_bass_kernel_spmd
from contextlib import ExitStack

F32 = mybir.dt.float32
BF16 = mybir.dt.bfloat16
AX = mybir.AxisListType.X
AF = mybir.ActivationFunctionType
OP = mybir.AluOpType
BF = ml_dtypes.bfloat16

D = 4096
L = 4096
NB = 8; CD = 32; HD = CD * NB
NH = 16
TB = 2048
EXP_SHIFT = -20.0
LN_EPS = 1e-5

ENGS = ("sync", "tensor", "vector", "scalar", "gpsimd")
RG = [[0, 1], [2, 3], [4, 5], [6, 7]]


def _host_consts():
    masks = sorted(range(NB), key=lambda m: (bin(m).count("1"), m))
    inv = {m: i for i, m in enumerate(masks)}
    C = np.zeros((NB, NB, NB), dtype=np.float64)
    for i, a in enumerate(masks):
        for j, b in enumerate(masks):
            sw, at = 0, a >> 1
            while at:
                sw += bin(at & b).count("1"); at >>= 1
            s = -1.0 if sw & 1 else 1.0
            C[i, j, inv[a ^ b]] = s
    grades = np.array([bin(m).count("1") for m in masks])
    rev = np.where(grades * (grades - 1) // 2 % 2 == 1, -1.0, 1.0)
    met = np.ones(NB)
    biv = C[:, :, [4, 5, 6]]
    scale = 1.0 / np.sqrt(CD * NB)
    D8 = scale * np.diag(rev * met)
    Bc = [rev[:, None] * biv[:, :, c] for c in range(3)]
    I16 = np.eye(16)
    dfull = np.kron(I16, D8).astype(np.float32)
    bfull = [np.kron(I16, b).astype(np.float32) for b in Bc]
    return dfull, bfull


class Ctr:
    def __init__(self, h):
        self.h = h
        self.n = 0
    def inc(self, k=1):
        self.n += k
        return (self.h, k)
    def now(self):
        return (self.h, self.n)


class Prog:
    def __init__(self, block):
        self.block = block
        self.bodies = {e: [] for e in ENGS}

    def E(self, eng, op, *args, incs=(), **kw):
        def run(e, op=op, args=args, kw=kw, incs=tuple(incs)):
            ins = getattr(e, op)(*args, **kw)
            for (h, k) in incs:
                ins.then_inc(h, k)
        self.bodies[eng].append(run)

    def W(self, eng, ctr, val=None):
        if isinstance(ctr, tuple):
            h, v = ctr
        else:
            h, v = ctr.h, (ctr.n if val is None else val)
        self.bodies[eng].append(lambda e, h=h, v=v: e.wait_ge(h, v))

    def flush(self):
        for name in ENGS:
            body = self.bodies[name]
            if not body:
                continue
            getattr(self.block, name)(
                lambda e, body=tuple(body): [f(e) for f in body])
            self.bodies[name].clear()


def build(stop_after="P3", debug_scratch=False):
    nc = bass.Bass("TRN2", target_bir_lowering=False, debug=False)

    x_d = nc.dram_tensor("x", [TB, D], F32, kind="ExternalInput").ap()
    wq_d = nc.dram_tensor("wq", [D, D], F32, kind="ExternalInput").ap()
    wk_d = nc.dram_tensor("wk", [D, D], F32, kind="ExternalInput").ap()
    wv_d = nc.dram_tensor("wv", [D, D], F32, kind="ExternalInput").ap()
    bq_d = nc.dram_tensor("bq", [128, 32], F32, kind="ExternalInput").ap()
    bk_d = nc.dram_tensor("bk", [128, 32], F32, kind="ExternalInput").ap()
    bv_d = nc.dram_tensor("bv", [128, 32], F32, kind="ExternalInput").ap()
    wf_d = nc.dram_tensor("wf", [D, D], F32, kind="ExternalInput").ap()
    bf_d = nc.dram_tensor("bfb", [1, D], F32, kind="ExternalInput").ap()
    gw_d = nc.dram_tensor("gw", [8, D], F32, kind="ExternalInput").ap()
    ics_d = nc.dram_tensor("ics", [8, D], F32, kind="ExternalInput").ap()
    ocs_d = nc.dram_tensor("ocs", [8, D], F32, kind="ExternalInput").ap()
    bwf_d = nc.dram_tensor("bwf", [1, 48], F32, kind="ExternalInput").ap()

    out_d = nc.dram_tensor("out", [TB, D], F32, kind="ExternalOutput").ap()

    wfT_d = nc.dram_tensor("wfT", [32 * 128, D], BF16).ap()
    qTs_d = nc.dram_tensor("qTs", [D, TB], BF16).ap()
    kTs_d = nc.dram_tensor("kTs", [D, TB], BF16).ap()
    vTs_d = nc.dram_tensor("vTs", [D, TB], BF16).ap()
    attns_d = nc.dram_tensor("attns", [TB, D], BF16).ap()
    x1s_d = nc.dram_tensor("x1s", [TB, D], F32).ap()
    cok = [nc.dram_tensor(f"cok{h}", [2 * HD, TB], BF16).ap() for h in range(NH)]
    cov = [nc.dram_tensor(f"cov{h}", [2 * HD, TB], BF16).ap() for h in range(NH)]
    dbg = {}
    if debug_scratch:
        for nm, ap_, shp, dt_ in (
                ("wfT", wfT_d, [32 * 128, D], BF16), ("qTs", qTs_d, [D, TB], BF16),
                ("kTs", kTs_d, [D, TB], BF16), ("vTs", vTs_d, [D, TB], BF16),
                ("attns", attns_d, [TB, D], BF16), ("x1s", x1s_d, [TB, D], F32),
                ("cok0", cok[0], [2 * HD, TB], BF16), ("cok15", cok[15], [2 * HD, TB], BF16)):
            dbg[nm] = (nc.dram_tensor("d_" + nm, shp, dt_, kind="ExternalOutput").ap(), ap_)

    dfull_np, bfull_np = _host_consts()
    ident_d = nc.inline_tensor(np.eye(128).astype(BF), name="identc").ap()
    ident8_d = nc.inline_tensor(np.eye(8).astype(BF), name="ident8c").ap()
    ones1f_d = nc.inline_tensor(np.ones((1, 128), np.float32), name="ones1f").ap()
    dfull_d = nc.inline_tensor(dfull_np, name="dfullc").ap()
    bfull_d = [nc.inline_tensor(bfull_np[c], name=f"bfull{c}c").ap() for c in range(3)]

    with ExitStack() as outer:
        OE = outer.enter_context
        ident = OE(nc.sbuf_tensor("identsb", [128, 128], BF16))
        ident8 = OE(nc.sbuf_tensor("ident8sb", [8, 8], BF16))
        ones1f = OE(nc.sbuf_tensor("ones1fsb", [1, 128], F32))
        G_bf = OE(nc.sbuf_tensor("G_bf", [128, NH * 128], BF16))
        bqe = OE(nc.sbuf_tensor("bqe", [128, 32], F32))
        bkc = OE(nc.sbuf_tensor("bkc", [128, 32], F32))
        bvc = OE(nc.sbuf_tensor("bvc", [128, 32], F32))
        cbias = OE(nc.sbuf_tensor("cbias", [128, 2], F32))

        block = OE(nc.Block())
        p = Prog(block)
        E, W = p.E, p.W

        qx = Ctr(OE(nc.semaphore("qx")))
        qg = Ctr(OE(nc.semaphore("qg")))
        mm = Ctr(OE(nc.semaphore("mm")))
        ev = Ctr(OE(nc.semaphore("ev")))
        dv = Ctr(OE(nc.semaphore("dv")))
        cc = Ctr(OE(nc.semaphore("cc")))
        allsems = [qx, qg, mm, ev, dv, cc]

        def phase_barrier():
            for e in ENGS:
                for s in allsems:
                    if s.n:
                        W(e, s)

        def finish():
            if debug_scratch:
                for e in ENGS:
                    for s_ in allsems:
                        if s_.n:
                            W(e, s_)
                for nm, (od, id_) in dbg.items():
                    E("gpsimd", "dma_start", out=od, in_=id_, incs=[qg.inc(16)])
            W("sync", qx)
            W("sync", qg)
            W("sync", mm)
            W("sync", ev)
            W("sync", dv)
            if cc.n:
                W("sync", cc)
            p.flush()


        with ExitStack() as ph:
            PE_ = ph.enter_context
            bwf = PE_(nc.sbuf_tensor("bwfsb", [1, 48], F32))
            bc48 = PE_(nc.sbuf_tensor("bc48", [128, 48], F32))
            gt = [PE_(nc.sbuf_tensor(f"gt{i}", [128, 128], F32)) for i in range(2)]
            dfull = PE_(nc.sbuf_tensor("dfullsb", [128, 128], F32))
            bfull = [PE_(nc.sbuf_tensor(f"bfullsb{c}", [128, 128], F32)) for c in range(3)]
            bq32 = PE_(nc.sbuf_tensor("bq32", [128, 32], F32))
            bqbf = PE_(nc.sbuf_tensor("bqbf", [128, 32], BF16))
            ps = [PE_(nc.psum_tensor(f"psW{i}", [128, 512], F32)) for i in range(4)]
            psS = PE_(nc.psum_tensor("psWS", [128, 128], F32))

            E("vector", "memset", cbias[:, 0:1], LN_EPS, incs=[dv.inc()])
            E("vector", "memset", cbias[:, 1:2], EXP_SHIFT, incs=[dv.inc()])
            W("scalar", dv)
            for dst, src in ((ident, ident_d), (ident8, ident8_d),
                             (ones1f, ones1f_d), (bwf, bwf_d),
                             (dfull, dfull_d), (bfull[0], bfull_d[0]),
                             (bfull[1], bfull_d[1]), (bfull[2], bfull_d[2]),
                             (bq32, bq_d), (bkc, bk_d), (bvc, bv_d)):
                E("sync", "dma_start", out=dst[:], in_=src, incs=[qx.inc(16)])
            ld_consts = qx.n

            W("tensor", qx, ld_consts)
            E("tensor", "matmul", psS[:, 0:48], ones1f[:], bwf[:],
              start=True, stop=True, incs=[mm.inc()])
            W("vector", mm)
            E("vector", "tensor_copy", bc48[:], psS[:, 0:48], incs=[dv.inc()])
            E("vector", "tensor_copy", bqbf[:], bq32[:], incs=[dv.inc()])

            for h in range(NH):
                c0 = 3 * h
                W("vector", dv)
                E("vector", "tensor_scalar", gt[0][:], bfull[0][:],
                  bc48[:, c0:c0 + 1], None, OP.mult, incs=[dv.inc()])
                W("vector", dv)
                E("vector", "scalar_tensor_tensor", gt[1][:], bfull[1][:],
                  bc48[:, c0 + 1:c0 + 2], gt[0][:], OP.mult, OP.add,
                  incs=[dv.inc()])
                W("vector", dv)
                E("vector", "scalar_tensor_tensor", gt[0][:], bfull[2][:],
                  bc48[:, c0 + 2:c0 + 3], gt[1][:], OP.mult, OP.add,
                  incs=[dv.inc()])
                W("vector", dv)
                E("vector", "tensor_add", gt[1][:], gt[0][:], dfull[:],
                  incs=[dv.inc()])
                W("vector", dv)
                E("vector", "tensor_copy", G_bf[:, h * 128:(h + 1) * 128],
                  gt[1][:], incs=[dv.inc()])
            g_done = dv.n

            W("tensor", dv, g_done)
            for m in range(32):
                h = m // 2
                E("tensor", "matmul", psS[:, 64:65], G_bf[:, h * 128:(h + 1) * 128],
                  bqbf[:, m:m + 1], start=True, stop=True, incs=[mm.inc()])
                W("vector", mm)
                E("vector", "tensor_copy", bqe[:, m:m + 1], psS[:, 64:65],
                  incs=[dv.inc()])
                W("tensor", dv)

            phase_barrier()
            p.flush()

            if stop_after == "W":
                finish()
                return nc

        with ExitStack() as ph:
            PE_ = ph.enter_context
            lnT = PE_(nc.sbuf_tensor("lnT", [128, 32 * TB], BF16))
            _xt = PE_(nc.sbuf_tensor("xt0", [128, D], F32))
            xt = [_xt, _xt]
            _ln1 = PE_(nc.sbuf_tensor("ln1_0", [128, D], BF16))
            ln1 = [_ln1, _ln1]
            bn48 = PE_(nc.sbuf_tensor("bn48", [128, 48], F32))
            bnag = PE_(nc.sbuf_tensor("bnag", [128, 2], F32))
            sdv = PE_(nc.sbuf_tensor("sdv", [128, 1], F32))
            rstd = PE_(nc.sbuf_tensor("rstd", [128, 1], F32))
            nbias = PE_(nc.sbuf_tensor("nbias", [128, 1], F32))
            wst = [PE_(nc.sbuf_tensor(f"wst{i}", [128, 32 * 128], BF16)) for i in range(2)]
            nat = [PE_(nc.sbuf_tensor(f"natw{i}", [128, D], BF16)) for i in range(2)]
            pstage = [PE_(nc.sbuf_tensor(f"pstg{i}", [128, 512], BF16)) for i in range(4)]
            psP = [PE_(nc.psum_tensor(f"psP{i}", [128, 512], F32)) for i in range(8)]
            psT = psP[:2]

            xt_guard = [None, None]
            ln1_guard = [None, None]
            tbank_guard = [None, None]
            nat_guard = [None, None]
            wst_guard = [None, None]
            bank_guard = [None] * 8
            pstage_guard = [None] * 4
            uig = [0]

            def emit_wf_unit(m):
                bi = uig[0] % 2
                uig[0] += 1
                if nat_guard[bi] is not None:
                    W("gpsimd", nat_guard[bi])
                E("gpsimd", "dma_start", out=nat[bi][:],
                  in_=wf_d[m * 128:(m + 1) * 128, :], incs=[qg.inc(16)])
                ld_here = qg.n
                if wst_guard[bi] is not None:
                    W("scalar", wst_guard[bi])
                W("tensor", qg, ld_here)
                for k in range(32):
                    bk_i = 2 + (k % 4)
                    sl = psP[bk_i]
                    if bank_guard[bk_i] is not None:
                        W("tensor", bank_guard[bk_i])
                    E("tensor", "matmul", sl[:, 0:128],
                      nat[bi][:, k * 128:(k + 1) * 128], ident[:],
                      start=True, stop=True, incs=[mm.inc()])
                    W("scalar", mm)
                    E("scalar", "sign", wst[bi][:, k * 128:(k + 1) * 128],
                      sl[:, 0:128], incs=[ev.inc()])
                    bank_guard[bk_i] = ev.now()
                nat_guard[bi] = mm.now()
                W("sync", ev)
                E("sync", "dma_start",
                  out=wfT_d.rearrange("(k p) r -> p k r", p=128)[:, :, m * 128:(m + 1) * 128],
                  in_=wst[bi].rearrange("p (k c) -> p k c", c=128),
                  incs=[qx.inc(16)])
                wst_guard[bi] = qx.now()

            for t in range(16):
                bi = t % 2
                if xt_guard[0] is not None:
                    W("sync", xt_guard[0])
                E("sync", "dma_start", out=xt[bi][:],
                  in_=x_d[t * 128:(t + 1) * 128, :], incs=[qx.inc(16)])
                xld = qx.n
                W("vector", qx, xld)
                W("vector", dv)
                for i8 in range(8):
                    E("vector", "bn_stats", bn48[:, i8 * 6:(i8 + 1) * 6],
                      xt[bi][:, i8 * 512:(i8 + 1) * 512],
                      incs=([dv.inc()] if i8 == 7 else []))
                W("vector", dv)
                E("vector", "bn_aggr", bnag[:], bn48[:], incs=[dv.inc()])
                W("scalar", dv)
                E("scalar", "activation", sdv[:], bnag[:, 1:2], AF.Sqrt,
                  bias=cbias[:, 0:1], scale=1.0, incs=[ev.inc()])
                W("vector", ev)
                E("vector", "reciprocal", rstd[:], sdv[:], incs=[dv.inc()])
                W("vector", dv)
                E("vector", "tensor_scalar", nbias[:], bnag[:, 0:1],
                  rstd[:], -1.0, OP.mult, OP.mult, incs=[dv.inc()])
                if ln1_guard[0] is not None:
                    W("scalar", ln1_guard[0])
                W("scalar", dv)
                E("scalar", "activation", ln1[bi][:], xt[bi][:], AF.Identity,
                  bias=nbias[:, 0:1], scale=rstd[:, 0:1], incs=[ev.inc()])
                ln_done = ev.n
                xt_guard[0] = ev.now()
                for k in range(32):
                    tbi = k % 2
                    if tbank_guard[tbi] is not None:
                        W("tensor", tbank_guard[tbi])
                    if k == 0:
                        W("tensor", ev, ln_done)
                    E("tensor", "matmul", psT[tbi][:, 0:128],
                      ln1[bi][:, k * 128:(k + 1) * 128], ident[:],
                      start=True, stop=True, incs=[mm.inc()])
                    eng, ctr, op = (("scalar", ev, "copy") if k % 2 == 0
                                    else ("vector", dv, "tensor_copy"))
                    W(eng, mm)
                    E(eng, op,
                      lnT[:, k * TB + t * 128: k * TB + t * 128 + 128],
                      psT[tbi][:, 0:128], incs=[ctr.inc()])
                    tbank_guard[tbi] = ctr.now()
                ln1_guard[0] = mm.now()
                emit_wf_unit(2 * t)
                emit_wf_unit(2 * t + 1)
            lnT_ev = ev.n
            lnT_dv = dv.n

            units = []
            for m in range(32):
                units.append(("k", m))
            for m in range(32):
                units.append(("v", m))
            for m in range(32):
                units.append(("q", m))
            srcs = {"k": wk_d, "v": wv_d, "q": wq_d}
            outs = {"k": kTs_d, "v": vTs_d, "q": qTs_d}
            biases = {"k": bkc, "v": bvc, "q": bqe}
            bank_guard[0] = (ev.h, lnT_ev)
            bank_guard[1] = (dv.h, lnT_dv)
            v_done = {}
            for ui, (jn, m) in enumerate(units):
                bi = uig[0] % 2
                uig[0] += 1
                bs = (ui % 2) * 4
                if nat_guard[bi] is not None:
                    W("gpsimd", nat_guard[bi])
                E("gpsimd", "dma_start", out=nat[bi][:],
                  in_=srcs[jn][m * 128:(m + 1) * 128, :], incs=[qg.inc(16)])
                ld_here = qg.n
                if wst_guard[bi] is not None:
                    W("scalar", wst_guard[bi])
                    W("vector", wst_guard[bi])
                W("tensor", qg, ld_here)
                rhs = (G_bf[:, (m // 2) * 128:(m // 2 + 1) * 128]
                       if jn == "q" else ident[:])
                for k in range(32):
                    bk_i = bs + (k % 4)
                    sl = psP[bk_i]
                    if bank_guard[bk_i] is not None:
                        W("tensor", bank_guard[bk_i])
                    E("tensor", "matmul", sl[:, 0:128],
                      nat[bi][:, k * 128:(k + 1) * 128], rhs,
                      start=True, stop=True, incs=[mm.inc()])
                    eng, ctr, op = (("scalar", ev, "copy") if k % 2 == 0
                                    else ("vector", dv, "tensor_copy"))
                    W(eng, mm)
                    E(eng, op, wst[bi][:, k * 128:(k + 1) * 128],
                      sl[:, 0:128], incs=[ctr.inc()])
                    bank_guard[bk_i] = ctr.now()
                nat_guard[bi] = mm.now()
                tr_ev, tr_dv = ev.n, dv.n
                W("tensor", ev, tr_ev)
                W("tensor", dv, tr_dv)
                if jn == "k" and m == 0:
                    W("tensor", ev, lnT_ev)
                    W("tensor", dv, lnT_dv)
                for tt in range(4):
                    if bank_guard[bs + tt] is not None:
                        W("tensor", bank_guard[bs + tt])
                for k in range(32):
                    for tt in range(4):
                        E("tensor", "matmul", psP[bs + tt][:],
                          wst[bi][:, k * 128:(k + 1) * 128],
                          lnT[:, k * TB + tt * 512: k * TB + tt * 512 + 512],
                          start=(k == 0), stop=(k == 31),
                          incs=([mm.inc()] if (k == 31 and tt == 3) else []))
                grp_done = mm.n
                for tt in range(4):
                    si = tt
                    if pstage_guard[si] is not None:
                        W("scalar", pstage_guard[si])
                    W("scalar", mm, grp_done)
                    E("scalar", "activation", pstage[si][:], psP[bs + tt][:],
                      AF.Identity, bias=biases[jn][:, m:m + 1], scale=1.0,
                      incs=[ev.inc()])
                    bank_guard[bs + tt] = ev.now()
                    W("sync", ev)
                    E("sync", "dma_start",
                      out=outs[jn][m * 128:(m + 1) * 128, tt * 512:(tt + 1) * 512],
                      in_=pstage[si][:], incs=[qx.inc(16)])
                    pstage_guard[si] = qx.now()
                wst_guard[bi] = mm.now()
                if jn == "v":
                    v_done[m] = qx.n
                    if m % 2 == 1:
                        h = m // 2
                        W("gpsimd", qx, v_done[m])
                        E("gpsimd", "collective_compute", "AllGather", OP.bypass,
                          replica_groups=RG,
                          ins=[kTs_d[2 * h * 128:(2 * h + 2) * 128, :].opt()],
                          outs=[cok[h].opt()], incs=[cc.inc()])
                        E("gpsimd", "collective_compute", "AllGather", OP.bypass,
                          replica_groups=RG,
                          ins=[vTs_d[2 * h * 128:(2 * h + 2) * 128, :].opt()],
                          outs=[cov[h].opt()], incs=[cc.inc()])
            phase_barrier()
            p.flush()

            if stop_after == "P1":
                finish()
                return nc

        with ExitStack() as ph:
            PE_ = ph.enter_context
            q_sb = [PE_(nc.sbuf_tensor(f"qsb{i}", [128, 2 * TB], BF16)) for i in range(2)]
            k_sb = [PE_(nc.sbuf_tensor(f"ksb{i}", [128, 2 * L], BF16)) for i in range(2)]
            vT_sb = [PE_(nc.sbuf_tensor(f"vTsb{i}", [128, 2 * L], BF16)) for i in range(2)]
            v_sb = [PE_(nc.sbuf_tensor(f"vsb{i}", [128, 32 * 257], BF16)) for i in range(2)]
            P_sb = [PE_(nc.sbuf_tensor(f"Psb{i}", [128, 32 * 512], BF16)) for i in range(2)]
            astg = [PE_(nc.sbuf_tensor(f"astg{i}", [128, 4 * 256], BF16)) for i in range(2)]
            rcol = [PE_(nc.sbuf_tensor(f"rcol{i}", [128, 1], F32)) for i in range(4)]
            psL = [PE_(nc.psum_tensor(f"psL{i}", [128, 512], F32)) for i in range(4)]
            psV = [PE_(nc.psum_tensor(f"psV{i}", [128, 512], F32)) for i in range(2)]
            psO = [PE_(nc.psum_tensor(f"psO{i}", [128, 257], F32)) for i in range(2)]

            for b_ in range(2):
                E("vector", "memset",
                  v_sb[b_].rearrange("p (kc c) -> p kc c", c=257)[:, :, 256:257],
                  1.0, incs=[dv.inc()])
            ones_done = dv.n

            qkv_guard = [None, None]
            P_guard = [None, None]
            psL_guard = [None] * 4
            psV_guard = [None] * 2
            psO_guard = [None] * 2
            astg_guard = [None, None]

            def load_head(h):
                bi = h % 2
                if qkv_guard[bi] is not None:
                    W("sync", qkv_guard[bi])
                W("sync", cc, 2 * (h + 1))
                E("sync", "dma_start",
                  out=q_sb[bi].rearrange("p (j t) -> p j t", j=2),
                  in_=qTs_d.rearrange("(m p) t -> p m t", p=128)[:, 2 * h:2 * h + 2, :],
                  incs=[qx.inc(16)])
                for r_ in range(2):
                    E("sync", "dma_start",
                      out=k_sb[bi].rearrange("p (c r t) -> p c r t", c=2, r=2)[:, :, r_, :],
                      in_=cok[h].rearrange("(r c p) t -> p r c t", c=2, p=128)[:, r_, :, :],
                      incs=[qx.inc(16)])
                    E("sync", "dma_start",
                      out=vT_sb[bi].rearrange("p (c r t) -> p c r t", c=2, r=2)[:, :, r_, :],
                      in_=cov[h].rearrange("(r c p) t -> p r c t", c=2, p=128)[:, r_, :, :],
                      incs=[qx.inc(16)])
                return qx.n

            h_loaded = [None] * (NH + 1)
            h_loaded[0] = load_head(0)

            for h in range(NH):
                bi = h % 2
                if h + 1 < NH:
                    h_loaded[h + 1] = load_head(h + 1)
                W("tensor", qx, h_loaded[h])
                if h == 0:
                    W("tensor", dv, ones_done)
                for j in range(2):
                    for kc in range(32):
                        tbi = kc % 2
                        if psV_guard[tbi] is not None:
                            W("tensor", psV_guard[tbi])
                        E("tensor", "matmul", psV[tbi][:, 0:128],
                          vT_sb[bi][:, j * L + kc * 128: j * L + kc * 128 + 128],
                          ident[:], start=True, stop=True, incs=[mm.inc()])
                        eng, ctr, op = (("scalar", ev, "copy") if kc % 2 == 0
                                        else ("vector", dv, "tensor_copy"))
                        W(eng, mm)
                        E(eng, op,
                          v_sb[bi][:, kc * 257 + j * 128: kc * 257 + j * 128 + 128],
                          psV[tbi][:, 0:128], incs=[ctr.inc()])
                        psV_guard[tbi] = ctr.now()
                vsb_ev = ev.n
                vsb_dv = dv.n

                for pt in range(2):
                    qts = [2 * pt, 2 * pt + 1]
                    for kc in range(32):
                        lbs = [0 * 2 + (kc % 2), 1 * 2 + (kc % 2)]
                        for q2 in range(2):
                            if psL_guard[lbs[q2]] is not None:
                                W("tensor", psL_guard[lbs[q2]])
                        for j in range(2):
                            for q2 in range(2):
                                E("tensor", "matmul", psL[lbs[q2]][:],
                                  k_sb[bi][:, j * L + kc * 128: j * L + kc * 128 + 128],
                                  q_sb[bi][:, j * TB + qts[q2] * 512: j * TB + qts[q2] * 512 + 512],
                                  start=(j == 0), stop=(j == 1),
                                  incs=([mm.inc()] if (j == 1 and q2 == 1) else []))
                        W("scalar", mm)
                        for q2 in range(2):
                            if kc == 0 and P_guard[q2] is not None:
                                W("scalar", P_guard[q2])
                            E("scalar", "activation",
                              P_sb[q2][:, kc * 512:(kc + 1) * 512],
                              psL[lbs[q2]][:],
                              AF.Exp, bias=cbias[:, 1:2], scale=1.0,
                              incs=[ev.inc()])
                            psL_guard[lbs[q2]] = ev.now()
                    exp_done = ev.n
                    W("tensor", ev, exp_done)
                    W("tensor", dv, vsb_dv)
                    W("tensor", ev, vsb_ev)
                    for q2 in range(2):
                        qt = qts[q2]
                        for qc in range(4):
                            ob = qc % 2
                            if psO_guard[ob] is not None:
                                W("tensor", psO_guard[ob])
                            for kc in range(32):
                                E("tensor", "matmul", psO[ob][:],
                                  P_sb[q2][:, kc * 512 + qc * 128: kc * 512 + qc * 128 + 128],
                                  v_sb[bi][:, kc * 257: kc * 257 + 257],
                                  start=(kc == 0), stop=(kc == 31),
                                  incs=([mm.inc()] if kc == 31 else []))
                            W("vector", mm)
                            W("vector", dv)
                            E("vector", "reciprocal", rcol[qc][:], psO[ob][:, 256:257],
                              incs=[dv.inc()])
                            W("vector", dv)
                            si = qt % 2
                            if qc == 0 and astg_guard[si] is not None:
                                W("vector", astg_guard[si])
                            E("vector", "tensor_scalar",
                              astg[si][:, qc * 256:(qc + 1) * 256],
                              psO[ob][:, 0:256], rcol[qc][:], None, OP.mult,
                              incs=[dv.inc()])
                            psO_guard[ob] = dv.now()
                        P_guard[q2] = mm.now()
                        W("sync", dv)
                        E("sync", "dma_start",
                          out=attns_d[qt * 512:(qt + 1) * 512,
                                      h * 256:(h + 1) * 256].rearrange("(g p) c -> p g c", p=128),
                          in_=astg[qt % 2].rearrange("p (g c) -> p g c", c=256),
                          incs=[qx.inc(16)])
                        astg_guard[qt % 2] = qx.now()
                qkv_guard[bi] = mm.now()
            phase_barrier()
            p.flush()

            if stop_after == "P2":
                finish()
                return nc

        with ExitStack() as ph:
            PE_ = ph.enter_context
            SBT = 1024
            XinT = PE_(nc.sbuf_tensor("XinT", [128, 32 * SBT], BF16))
            x1 = [PE_(nc.sbuf_tensor(f"x1_{i}", [128, D], F32)) for i in range(2)]
            ln2 = [PE_(nc.sbuf_tensor(f"ln2_{i}", [128, D], BF16)) for i in range(2)]
            bn48b = PE_(nc.sbuf_tensor("bn48b", [128, 48], F32))
            bnagb = PE_(nc.sbuf_tensor("bnagb", [128, 2], F32))
            sdvb = PE_(nc.sbuf_tensor("sdvb", [128, 1], F32))
            rstdb = PE_(nc.sbuf_tensor("rstdb", [128, 1], F32))
            nbiasb = PE_(nc.sbuf_tensor("nbiasb", [128, 1], F32))
            biasb = PE_(nc.sbuf_tensor("biasb", [128, D], BF16))
            bfrow = PE_(nc.sbuf_tensor("bfrow", [1, 1024], F32))
            gwbf = PE_(nc.sbuf_tensor("gwbf", [8, D], BF16))
            gwT = PE_(nc.sbuf_tensor("gwT", [128, 32 * 8], BF16))
            icsbf = PE_(nc.sbuf_tensor("icsbf", [8, D], BF16))
            ocsbf = PE_(nc.sbuf_tensor("ocsbf", [8, D], BF16))
            rT = PE_(nc.sbuf_tensor("rT", [8, TB], BF16))
            nmax8 = PE_(nc.sbuf_tensor("nmax8", [128, 1], F32))
            er8 = PE_(nc.sbuf_tensor("er8", [128, 8], F32))
            rs8 = PE_(nc.sbuf_tensor("rs8", [128, 1], F32))
            rr8 = PE_(nc.sbuf_tensor("rr8", [128, 1], F32))
            rn8 = PE_(nc.sbuf_tensor("rn8", [128, 8], BF16))
            wf_all = [PE_(nc.sbuf_tensor(f"wfall{i}", [128, 32 * 256], BF16)) for i in range(2)]
            os_all = PE_(nc.sbuf_tensor("os_all", [128, 8 * 256], F32))
            x1b_sb = [PE_(nc.sbuf_tensor(f"x1b{i}", [128, 256], F32)) for i in range(2)]
            t1_sb = [PE_(nc.sbuf_tensor(f"t1s{i}", [128, 256], F32)) for i in range(2)]
            outf = [PE_(nc.sbuf_tensor(f"outf{i}", [128, 256], F32)) for i in range(2)]
            psA = [PE_(nc.psum_tensor(f"psA{i}", [128, 256], F32)) for i in range(4)]
            psB = [PE_(nc.psum_tensor(f"psB{i}", [128, 256], F32)) for i in range(4)]

            E("gpsimd", "dma_start", out=gwbf[:], in_=gw_d, incs=[qg.inc(16)])
            E("gpsimd", "dma_start", out=icsbf[:], in_=ics_d, incs=[qg.inc(16)])
            E("gpsimd", "dma_start", out=ocsbf[:], in_=ocs_d, incs=[qg.inc(16)])
            prep_qg = qg.n
            bb_bank = [None] * 4
            for part in range(4):
                W("sync", mm)
                E("sync", "dma_start", out=bfrow[:],
                  in_=bf_d[:, part * 1024:(part + 1) * 1024], incs=[qx.inc(16)])
                W("tensor", qx, qx.n)
                for i4 in range(4):
                    i = part * 4 + i4
                    pb = psA[i % 4]
                    if bb_bank[i % 4] is not None:
                        W("tensor", bb_bank[i % 4])
                    E("tensor", "matmul", pb[:, 0:256], ones1f[:],
                      bfrow[:, i4 * 256:(i4 + 1) * 256], start=True, stop=True,
                      incs=[mm.inc()])
                    W("vector", mm)
                    E("vector", "tensor_copy", biasb[:, i * 256:(i + 1) * 256],
                      pb[:, 0:256], incs=[dv.inc()])
                    bb_bank[i % 4] = dv.now()
            W("tensor", qg, prep_qg)
            gw_bank = [None, None]
            for k in range(32):
                pb = psB[k % 2]
                if gw_bank[k % 2] is not None:
                    W("tensor", gw_bank[k % 2])
                E("tensor", "matmul", pb[:, 0:8], gwbf[:, k * 128:(k + 1) * 128],
                  ident8[:], start=True, stop=True, incs=[mm.inc()])
                W("scalar", mm)
                E("scalar", "copy", gwT[:, k * 8:(k + 1) * 8], pb[:, 0:8],
                  incs=[ev.inc()])
                gw_bank[k % 2] = ev.now()
            gw_done = ev.n

            x1_guard = [None, None]
            ln2_guard = [None, None]
            t3bank = [bb_bank[0], bb_bank[1]]
            isbank = [gw_bank[0], gw_bank[1]]
            rbank = bb_bank[2]
            rtbank = bb_bank[3]
            wfall_guard = [None, None]
            osb_guard = None
            mainbank = [None] * 7
            x1b_guard = [None, None]
            t1_guard = [None, None]
            outf_guard = [None, None]
            osa_guard = [None] * 8

            for sb in range(2):
                for g_ in mainbank + [osb_guard]:
                    if g_ is not None:
                        W("tensor", g_)
                for t in range(8):
                    tt = sb * 8 + t
                    bi = t % 2
                    r0 = tt * 128
                    if x1_guard[bi] is not None:
                        W("sync", x1_guard[bi])
                    E("sync", "dma_start", out=x1[bi][:],
                      in_=x_d[r0:r0 + 128, :], incs=[qx.inc(16)])
                    if ln2_guard[bi] is not None:
                        W("sync", ln2_guard[bi])
                    E("sync", "dma_start", out=ln2[bi][:],
                      in_=attns_d[r0:r0 + 128, :], incs=[qx.inc(16)])
                    t_ld = qx.n
                    W("vector", qx, t_ld)
                    W("vector", dv)
                    E("vector", "tensor_add", x1[bi][:], x1[bi][:], ln2[bi][:],
                      incs=[dv.inc()])
                    W("vector", dv)
                    for i8 in range(8):
                        E("vector", "bn_stats", bn48b[:, i8 * 6:(i8 + 1) * 6],
                          x1[bi][:, i8 * 512:(i8 + 1) * 512],
                          incs=([dv.inc()] if i8 == 7 else []))
                    W("vector", dv)
                    E("vector", "bn_aggr", bnagb[:], bn48b[:], incs=[dv.inc()])
                    W("scalar", dv)
                    E("scalar", "activation", sdvb[:], bnagb[:, 1:2], AF.Sqrt,
                      bias=cbias[:, 0:1], scale=1.0, incs=[ev.inc()])
                    W("vector", ev)
                    E("vector", "reciprocal", rstdb[:], sdvb[:], incs=[dv.inc()])
                    W("vector", dv)
                    E("vector", "tensor_scalar", nbiasb[:], bnagb[:, 0:1],
                      rstdb[:], -1.0, OP.mult, OP.mult, incs=[dv.inc()])
                    if ln2_guard[bi] is not None:
                        W("scalar", ln2_guard[bi])
                    W("scalar", dv)
                    E("scalar", "activation", ln2[bi][:], x1[bi][:], AF.Identity,
                      bias=nbiasb[:, 0:1], scale=rstdb[:, 0:1], incs=[ev.inc()])
                    ln2_done = ev.n
                    W("vector", ev, ln2_done)
                    W("vector", dv)
                    E("vector", "tensor_add", x1[bi][:], x1[bi][:], biasb[:],
                      incs=[dv.inc()])
                    W("sync", dv)
                    E("sync", "dma_start", out=x1s_d[tt * 128:(tt + 1) * 128, :],
                      in_=x1[bi][:], incs=[qx.inc(16)])
                    x1_guard[bi] = qx.now()
                    for k in range(32):
                        tbi = k % 2
                        if t3bank[tbi] is not None:
                            W("tensor", t3bank[tbi])
                        if k == 0:
                            W("tensor", ev, ln2_done)
                        E("tensor", "matmul", psA[tbi][:, 0:128],
                          ln2[bi][:, k * 128:(k + 1) * 128], ident[:],
                          start=True, stop=True, incs=[mm.inc()])
                        eng, ctr, op = (("scalar", ev, "copy") if k % 2 == 0
                                        else ("vector", dv, "tensor_copy"))
                        W(eng, mm)
                        E(eng, op,
                          XinT[:, k * SBT + t * 128: k * SBT + t * 128 + 128],
                          psA[tbi][:, 0:128], incs=[ctr.inc()])
                        t3bank[tbi] = ctr.now()
                    ln2_guard[bi] = mm.now()
                    xinT_ev, xinT_dv = ev.n, dv.n
                    if rbank is not None:
                        W("tensor", rbank)
                    W("tensor", ev, xinT_ev)
                    W("tensor", dv, xinT_dv)
                    for k in range(32):
                        E("tensor", "matmul", psA[2][:, 0:8],
                          XinT[:, k * SBT + t * 128: k * SBT + t * 128 + 128],
                          gwT[:, k * 8:(k + 1) * 8],
                          start=(k == 0), stop=(k == 31),
                          incs=([mm.inc()] if k == 31 else []))
                    router_done = mm.n
                    W("vector", mm, router_done)
                    W("vector", dv)
                    E("vector", "reduce_max", nmax8[:], psA[2][:, 0:8], AX,
                      negate=True, incs=[dv.inc()])
                    W("scalar", dv)
                    W("scalar", ev, gw_done)
                    E("scalar", "activation", er8[:], psA[2][:, 0:8], AF.Exp,
                      bias=nmax8[:, 0:1], scale=1.0, incs=[ev.inc()])
                    rbank = ev.now()
                    W("vector", ev)
                    E("vector", "reduce_sum", rs8[:], er8[:], AX, incs=[dv.inc()])
                    W("vector", dv)
                    E("vector", "reciprocal", rr8[:], rs8[:], incs=[dv.inc()])
                    W("vector", dv)
                    E("vector", "tensor_scalar", rn8[:], er8[:], rr8[:], None,
                      OP.mult, incs=[dv.inc()])
                    rn_done = dv.n
                    if rtbank is not None:
                        W("tensor", rtbank)
                    W("tensor", dv, rn_done)
                    E("tensor", "matmul", psA[3][0:8, 0:128], rn8[:], ident[:],
                      start=True, stop=True, incs=[mm.inc()])
                    W("scalar", mm)
                    E("scalar", "copy", rT[:, tt * 128:(tt + 1) * 128],
                      psA[3][0:8, 0:128], incs=[ev.inc()])
                    rtbank = ev.now()
                    rT_done = ev.n
                    W("tensor", ev, rT_done)
                    for k in range(32):
                        ib = k % 2
                        if isbank[ib] is not None:
                            W("tensor", isbank[ib])
                        E("tensor", "matmul", psB[ib][:, 0:128],
                          icsbf[:, k * 128:(k + 1) * 128],
                          rT[:, tt * 128:(tt + 1) * 128],
                          start=True, stop=True, incs=[mm.inc()])
                        W("vector", mm)
                        E("vector", "tensor_mul",
                          XinT[:, k * SBT + t * 128: k * SBT + t * 128 + 128],
                          XinT[:, k * SBT + t * 128: k * SBT + t * 128 + 128],
                          psB[ib][:, 0:128], incs=[dv.inc()])
                        isbank[ib] = dv.now()
                xin_ready_dv = dv.n
                xin_ready_ev = ev.n

                W("tensor", dv, xin_ready_dv)
                W("tensor", ev, xin_ready_ev)
                banks7 = psA + psB[:3]
                for oc in range(16):
                    gc = sb * 16 + oc
                    wb = gc % 2
                    if wfall_guard[wb] is not None:
                        W("gpsimd", wfall_guard[wb])
                    for q4 in range(4):
                        E("gpsimd", "dma_start",
                          out=wf_all[wb].rearrange("p (k r) -> p k r", r=256)[
                              :, q4 * 8:(q4 + 1) * 8, :],
                          in_=wfT_d.rearrange("(k p) r -> p k r", p=128)[
                              :, q4 * 8:(q4 + 1) * 8, oc * 256:(oc + 1) * 256],
                          incs=[qg.inc(16)])
                    wf_ld = qg.n
                    W("tensor", qg, wf_ld)
                    for tc in range(8):
                        tg = sb * 8 + tc
                        if osb_guard is not None:
                            W("tensor", osb_guard)
                        E("tensor", "matmul", psB[3][:, 0:256],
                          rT[:, tg * 128:(tg + 1) * 128],
                          ocsbf[:, oc * 256:(oc + 1) * 256],
                          start=True, stop=True, incs=[mm.inc()])
                        W("scalar", mm)
                        if osa_guard[tc] is not None:
                            W("scalar", osa_guard[tc])
                        E("scalar", "copy", os_all[:, tc * 256:(tc + 1) * 256],
                          psB[3][:, 0:256], incs=[ev.inc()])
                        osb_guard = ev.now()
                        os_ev = ev.n
                        bidx = (gc * 8 + tc) % 7
                        bank = banks7[bidx]
                        if mainbank[bidx] is not None:
                            W("tensor", mainbank[bidx])
                        for k in range(32):
                            E("tensor", "matmul", bank[:, 0:256],
                              XinT[:, k * SBT + tc * 128: k * SBT + tc * 128 + 128],
                              wf_all[wb][:, k * 256:(k + 1) * 256],
                              start=(k == 0), stop=(k == 31),
                              incs=([mm.inc()] if k == 31 else []))
                        main_done = mm.n
                        eb = tc % 2
                        if x1b_guard[eb] is not None:
                            W("sync", x1b_guard[eb])
                        E("sync", "dma_start", out=x1b_sb[eb][:],
                          in_=x1s_d[tg * 128:(tg + 1) * 128,
                                    oc * 256:(oc + 1) * 256],
                          incs=[qx.inc(16)])
                        eld = qx.n
                        W("vector", mm, main_done)
                        W("vector", qx, eld)
                        W("vector", ev, os_ev)
                        if t1_guard[eb] is not None:
                            W("vector", t1_guard[eb])
                        E("vector", "tensor_mul", t1_sb[eb][:],
                          bank[:, 0:256], os_all[:, tc * 256:(tc + 1) * 256],
                          incs=[dv.inc()])
                        osa_guard[tc] = dv.now()
                        mainbank[bidx] = dv.now()
                        W("vector", dv)
                        if outf_guard[eb] is not None:
                            W("vector", outf_guard[eb])
                        E("vector", "tensor_add", outf[eb][:], t1_sb[eb][:],
                          x1b_sb[eb][:], incs=[dv.inc()])
                        t1_guard[eb] = dv.now()
                        x1b_guard[eb] = dv.now()
                        W("sync", dv)
                        E("sync", "dma_start",
                          out=out_d[tg * 128:(tg + 1) * 128,
                                    oc * 256:(oc + 1) * 256],
                          in_=outf[eb][:], incs=[qx.inc(16)])
                        outf_guard[eb] = qx.now()
                    wfall_guard[wb] = mm.now()
            if debug_scratch:
                dbg["rT"] = (nc.dram_tensor("d_rT", [8, TB], BF16,
                                            kind="ExternalOutput").ap(), rT[:])
                dbg["XinT"] = (nc.dram_tensor("d_XinT", [128, 32 * SBT], BF16,
                                              kind="ExternalOutput").ap(), XinT[:])
                dbg["osall"] = (nc.dram_tensor("d_osall", [128, 8 * 256], F32,
                                               kind="ExternalOutput").ap(), os_all[:])
            finish()
        return nc


def _layout_inputs(inp, core):
    b = core // 2
    hg = core % 2
    ts = slice(hg * TB, (hg + 1) * TB)
    m = {
        "x": np.ascontiguousarray(np.asarray(inp["L0_attn_x"][b][ts])),
        "wq": np.ascontiguousarray(np.asarray(inp["L0_attn_Wq"])),
        "wk": np.ascontiguousarray(np.asarray(inp["L0_attn_Wk"])),
        "wv": np.ascontiguousarray(np.asarray(inp["L0_attn_Wv"])),
        "bq": np.ascontiguousarray(np.asarray(inp["L0_attn_bq"]).reshape(32, 128).T),
        "bk": np.ascontiguousarray(np.asarray(inp["L0_attn_bk"]).reshape(32, 128).T),
        "bv": np.ascontiguousarray(np.asarray(inp["L0_attn_bv"]).reshape(32, 128).T),
        "wf": np.ascontiguousarray(np.asarray(inp["L0_ffn_weight"])),
        "bfb": np.ascontiguousarray(np.asarray(inp["L0_ffn_bias"]).reshape(1, D)),
        "gw": np.ascontiguousarray(np.asarray(inp["L0_ffn_gate_w"])),
        "ics": np.ascontiguousarray(np.asarray(inp["L0_ffn_in_channel_scale"])),
        "ocs": np.ascontiguousarray(np.asarray(inp["L0_ffn_out_channel_scale"])),
        "bwf": np.ascontiguousarray(np.asarray(inp["L0_attn_bias_weight"]).reshape(1, 48)),
    }
    return m


def kernel(**inputs):
    nc = build()
    in_maps = [_layout_inputs(inputs, c) for c in range(8)]
    res = run_bass_kernel_spmd(nc, in_maps, core_ids=list(range(8)))
    out = np.empty((4, L, D), dtype=np.float32)
    for c in range(8):
        b, hg = c // 2, c % 2
        out[b, hg * TB:(hg + 1) * TB, :] = res.results[c]["out"]
    return out


# version 04 (final)
# speedup vs baseline: 1.0218x; 1.0218x over previous
import numpy as np
import ml_dtypes
import concourse.bass as bass
import concourse.mybir as mybir
from concourse.bass_utils import run_bass_kernel_spmd
from contextlib import ExitStack

F32 = mybir.dt.float32
BF16 = mybir.dt.bfloat16
AX = mybir.AxisListType.X
AF = mybir.ActivationFunctionType
OP = mybir.AluOpType
BF = ml_dtypes.bfloat16

D = 4096
L = 4096
NB = 8; CD = 32; HD = CD * NB
NH = 16
TB = 2048
EXP_SHIFT = -20.0
LN_EPS = 1e-5

ENGS = ("sync", "tensor", "vector", "scalar", "gpsimd")
RG = [[0, 1], [2, 3], [4, 5], [6, 7]]


def _host_consts():
    masks = sorted(range(NB), key=lambda m: (bin(m).count("1"), m))
    inv = {m: i for i, m in enumerate(masks)}
    C = np.zeros((NB, NB, NB), dtype=np.float64)
    for i, a in enumerate(masks):
        for j, b in enumerate(masks):
            sw, at = 0, a >> 1
            while at:
                sw += bin(at & b).count("1"); at >>= 1
            s = -1.0 if sw & 1 else 1.0
            C[i, j, inv[a ^ b]] = s
    grades = np.array([bin(m).count("1") for m in masks])
    rev = np.where(grades * (grades - 1) // 2 % 2 == 1, -1.0, 1.0)
    met = np.ones(NB)
    biv = C[:, :, [4, 5, 6]]
    scale = 1.0 / np.sqrt(CD * NB)
    D8 = scale * np.diag(rev * met)
    Bc = [rev[:, None] * biv[:, :, c] for c in range(3)]
    I16 = np.eye(16)
    dfull = np.kron(I16, D8).astype(np.float32)
    bfull = [np.kron(I16, b).astype(np.float32) for b in Bc]
    return dfull, bfull


class Ctr:
    def __init__(self, h):
        self.h = h
        self.n = 0
    def inc(self, k=1):
        self.n += k
        return (self.h, k)
    def now(self):
        return (self.h, self.n)


class Prog:
    def __init__(self, block):
        self.block = block
        self.bodies = {e: [] for e in ENGS}

    def E(self, eng, op, *args, incs=(), **kw):
        def run(e, op=op, args=args, kw=kw, incs=tuple(incs)):
            ins = getattr(e, op)(*args, **kw)
            for (h, k) in incs:
                ins.then_inc(h, k)
        self.bodies[eng].append(run)

    def W(self, eng, ctr, val=None):
        if isinstance(ctr, tuple):
            h, v = ctr
        else:
            h, v = ctr.h, (ctr.n if val is None else val)
        self.bodies[eng].append(lambda e, h=h, v=v: e.wait_ge(h, v))

    def flush(self):
        for name in ENGS:
            body = self.bodies[name]
            if not body:
                continue
            getattr(self.block, name)(
                lambda e, body=tuple(body): [f(e) for f in body])
            self.bodies[name].clear()


def build(stop_after="P3", debug_scratch=False):
    nc = bass.Bass("TRN2", target_bir_lowering=False, debug=False)

    x_d = nc.dram_tensor("x", [TB, D], F32, kind="ExternalInput").ap()
    wq_d = nc.dram_tensor("wq", [D, D], F32, kind="ExternalInput").ap()
    wk_d = nc.dram_tensor("wk", [D, D], F32, kind="ExternalInput").ap()
    wv_d = nc.dram_tensor("wv", [D, D], F32, kind="ExternalInput").ap()
    bq_d = nc.dram_tensor("bq", [128, 32], F32, kind="ExternalInput").ap()
    bk_d = nc.dram_tensor("bk", [128, 32], F32, kind="ExternalInput").ap()
    bv_d = nc.dram_tensor("bv", [128, 32], F32, kind="ExternalInput").ap()
    wf_d = nc.dram_tensor("wf", [D, D], F32, kind="ExternalInput").ap()
    bf_d = nc.dram_tensor("bfb", [1, D], F32, kind="ExternalInput").ap()
    gw_d = nc.dram_tensor("gw", [8, D], F32, kind="ExternalInput").ap()
    ics_d = nc.dram_tensor("ics", [8, D], F32, kind="ExternalInput").ap()
    ocs_d = nc.dram_tensor("ocs", [8, D], F32, kind="ExternalInput").ap()
    bwf_d = nc.dram_tensor("bwf", [1, 48], F32, kind="ExternalInput").ap()

    out_d = nc.dram_tensor("out", [TB, D], F32, kind="ExternalOutput").ap()

    wfT_d = nc.dram_tensor("wfT", [32 * 128, D], BF16).ap()
    qTs_d = nc.dram_tensor("qTs", [D, TB], BF16).ap()
    kTs_d = nc.dram_tensor("kTs", [D, TB], BF16).ap()
    vTs_d = nc.dram_tensor("vTs", [D, TB], BF16).ap()
    attns_d = nc.dram_tensor("attns", [TB, D], BF16).ap()
    x1s_d = nc.dram_tensor("x1s", [TB, D], F32).ap()
    cok = [nc.dram_tensor(f"cok{h}", [2 * HD, TB], BF16).ap() for h in range(NH)]
    cov = [nc.dram_tensor(f"cov{h}", [2 * HD, TB], BF16).ap() for h in range(NH)]
    dbg = {}
    if debug_scratch:
        for nm, ap_, shp, dt_ in (
                ("wfT", wfT_d, [32 * 128, D], BF16), ("qTs", qTs_d, [D, TB], BF16),
                ("kTs", kTs_d, [D, TB], BF16), ("vTs", vTs_d, [D, TB], BF16),
                ("attns", attns_d, [TB, D], BF16), ("x1s", x1s_d, [TB, D], F32),
                ("cok0", cok[0], [2 * HD, TB], BF16), ("cok15", cok[15], [2 * HD, TB], BF16)):
            dbg[nm] = (nc.dram_tensor("d_" + nm, shp, dt_, kind="ExternalOutput").ap(), ap_)

    dfull_np, bfull_np = _host_consts()
    ident_d = nc.inline_tensor(np.eye(128).astype(BF), name="identc").ap()
    ident8_d = nc.inline_tensor(np.eye(8).astype(BF), name="ident8c").ap()
    ones1f_d = nc.inline_tensor(np.ones((1, 128), np.float32), name="ones1f").ap()
    dfull_d = nc.inline_tensor(dfull_np, name="dfullc").ap()
    bfull_d = [nc.inline_tensor(bfull_np[c], name=f"bfull{c}c").ap() for c in range(3)]

    with ExitStack() as outer:
        OE = outer.enter_context
        ident = OE(nc.sbuf_tensor("identsb", [128, 128], BF16))
        ident8 = OE(nc.sbuf_tensor("ident8sb", [8, 8], BF16))
        ones1f = OE(nc.sbuf_tensor("ones1fsb", [1, 128], F32))
        G_bf = OE(nc.sbuf_tensor("G_bf", [128, NH * 128], BF16))
        bqe = OE(nc.sbuf_tensor("bqe", [128, 32], F32))
        bkc = OE(nc.sbuf_tensor("bkc", [128, 32], F32))
        bvc = OE(nc.sbuf_tensor("bvc", [128, 32], F32))
        cbias = OE(nc.sbuf_tensor("cbias", [128, 2], F32))

        block = OE(nc.Block())
        p = Prog(block)
        E, W = p.E, p.W

        qx = Ctr(OE(nc.semaphore("qx")))
        qg = Ctr(OE(nc.semaphore("qg")))
        mm = Ctr(OE(nc.semaphore("mm")))
        ev = Ctr(OE(nc.semaphore("ev")))
        dv = Ctr(OE(nc.semaphore("dv")))
        cc = Ctr(OE(nc.semaphore("cc")))
        allsems = [qx, qg, mm, ev, dv, cc]

        def phase_barrier():
            for e in ENGS:
                for s in allsems:
                    if s.n:
                        W(e, s)

        def finish():
            if debug_scratch:
                for e in ENGS:
                    for s_ in allsems:
                        if s_.n:
                            W(e, s_)
                for nm, (od, id_) in dbg.items():
                    E("gpsimd", "dma_start", out=od, in_=id_, incs=[qg.inc(16)])
            W("sync", qx)
            W("sync", qg)
            W("sync", mm)
            W("sync", ev)
            W("sync", dv)
            if cc.n:
                W("sync", cc)
            p.flush()


        with ExitStack() as ph:
            PE_ = ph.enter_context
            bwf = PE_(nc.sbuf_tensor("bwfsb", [1, 48], F32))
            bc48 = PE_(nc.sbuf_tensor("bc48", [128, 48], F32))
            gt = [PE_(nc.sbuf_tensor(f"gt{i}", [128, 128], F32)) for i in range(2)]
            dfull = PE_(nc.sbuf_tensor("dfullsb", [128, 128], F32))
            bfull = [PE_(nc.sbuf_tensor(f"bfullsb{c}", [128, 128], F32)) for c in range(3)]
            bq32 = PE_(nc.sbuf_tensor("bq32", [128, 32], F32))
            bqbf = PE_(nc.sbuf_tensor("bqbf", [128, 32], BF16))
            ps = [PE_(nc.psum_tensor(f"psW{i}", [128, 512], F32)) for i in range(4)]
            psS = PE_(nc.psum_tensor("psWS", [128, 128], F32))

            E("vector", "memset", cbias[:, 0:1], LN_EPS, incs=[dv.inc()])
            E("vector", "memset", cbias[:, 1:2], EXP_SHIFT, incs=[dv.inc()])
            W("scalar", dv)
            for dst, src in ((ident, ident_d), (ident8, ident8_d),
                             (ones1f, ones1f_d), (bwf, bwf_d),
                             (dfull, dfull_d), (bfull[0], bfull_d[0]),
                             (bfull[1], bfull_d[1]), (bfull[2], bfull_d[2]),
                             (bq32, bq_d), (bkc, bk_d), (bvc, bv_d)):
                E("sync", "dma_start", out=dst[:], in_=src, incs=[qx.inc(16)])
            ld_consts = qx.n

            W("tensor", qx, ld_consts)
            E("tensor", "matmul", psS[:, 0:48], ones1f[:], bwf[:],
              start=True, stop=True, incs=[mm.inc()])
            W("vector", mm)
            E("vector", "tensor_copy", bc48[:], psS[:, 0:48], incs=[dv.inc()])
            E("vector", "tensor_copy", bqbf[:], bq32[:], incs=[dv.inc()])

            for h in range(NH):
                c0 = 3 * h
                W("vector", dv)
                E("vector", "tensor_scalar", gt[0][:], bfull[0][:],
                  bc48[:, c0:c0 + 1], None, OP.mult, incs=[dv.inc()])
                W("vector", dv)
                E("vector", "scalar_tensor_tensor", gt[1][:], bfull[1][:],
                  bc48[:, c0 + 1:c0 + 2], gt[0][:], OP.mult, OP.add,
                  incs=[dv.inc()])
                W("vector", dv)
                E("vector", "scalar_tensor_tensor", gt[0][:], bfull[2][:],
                  bc48[:, c0 + 2:c0 + 3], gt[1][:], OP.mult, OP.add,
                  incs=[dv.inc()])
                W("vector", dv)
                E("vector", "tensor_add", gt[1][:], gt[0][:], dfull[:],
                  incs=[dv.inc()])
                W("vector", dv)
                E("vector", "tensor_copy", G_bf[:, h * 128:(h + 1) * 128],
                  gt[1][:], incs=[dv.inc()])
            g_done = dv.n

            W("tensor", dv, g_done)
            for m in range(32):
                h = m // 2
                E("tensor", "matmul", psS[:, 64:65], G_bf[:, h * 128:(h + 1) * 128],
                  bqbf[:, m:m + 1], start=True, stop=True, incs=[mm.inc()])
                W("vector", mm)
                E("vector", "tensor_copy", bqe[:, m:m + 1], psS[:, 64:65],
                  incs=[dv.inc()])
                W("tensor", dv)

            phase_barrier()
            p.flush()

            if stop_after == "W":
                finish()
                return nc

        with ExitStack() as ph:
            PE_ = ph.enter_context
            lnT = PE_(nc.sbuf_tensor("lnT", [128, 32 * TB], BF16))
            _xt = PE_(nc.sbuf_tensor("xt0", [128, D], F32))
            xt = [_xt, _xt]
            _ln1 = PE_(nc.sbuf_tensor("ln1_0", [128, D], BF16))
            ln1 = [_ln1, _ln1]
            bn48 = PE_(nc.sbuf_tensor("bn48", [128, 48], F32))
            bnag = PE_(nc.sbuf_tensor("bnag", [128, 2], F32))
            sdv = PE_(nc.sbuf_tensor("sdv", [128, 1], F32))
            rstd = PE_(nc.sbuf_tensor("rstd", [128, 1], F32))
            nbias = PE_(nc.sbuf_tensor("nbias", [128, 1], F32))
            wst = [PE_(nc.sbuf_tensor(f"wst{i}", [128, 32 * 128], BF16)) for i in range(2)]
            nat = [PE_(nc.sbuf_tensor(f"natw{i}", [128, D], BF16)) for i in range(2)]
            pstage = [PE_(nc.sbuf_tensor(f"pstg{i}", [128, 512], BF16)) for i in range(4)]
            psP = [PE_(nc.psum_tensor(f"psP{i}", [128, 512], F32)) for i in range(8)]
            psT = psP[:2]

            xt_guard = [None, None]
            ln1_guard = [None, None]
            tbank_guard = [None, None]
            nat_guard = [None, None]
            wst_guard = [None, None]
            bank_guard = [None] * 8
            pstage_guard = [None] * 4
            uig = [0]

            def emit_wf_unit(m):
                bi = uig[0] % 2
                uig[0] += 1
                if nat_guard[bi] is not None:
                    W("gpsimd", nat_guard[bi])
                E("gpsimd", "dma_start", out=nat[bi][:],
                  in_=wf_d[m * 128:(m + 1) * 128, :], incs=[qg.inc(16)])
                ld_here = qg.n
                if wst_guard[bi] is not None:
                    W("scalar", wst_guard[bi])
                    W("vector", wst_guard[bi])
                W("tensor", qg, ld_here)
                for k in range(32):
                    bk_i = 2 + (k % 4)
                    sl = psP[bk_i]
                    if bank_guard[bk_i] is not None:
                        W("tensor", bank_guard[bk_i])
                    E("tensor", "matmul", sl[:, 0:128],
                      nat[bi][:, k * 128:(k + 1) * 128], ident[:],
                      start=True, stop=True, incs=[mm.inc()])
                    eng, ctr, op = (("scalar", ev, "copy") if k % 2 == 0
                                    else ("vector", dv, "tensor_copy"))
                    W(eng, mm)
                    E(eng, op, wst[bi][:, k * 128:(k + 1) * 128],
                      sl[:, 0:128], incs=[ctr.inc()])
                    bank_guard[bk_i] = ctr.now()
                nat_guard[bi] = mm.now()
                W("sync", ev)
                W("sync", dv)
                E("sync", "dma_start",
                  out=wfT_d.rearrange("(k p) r -> p k r", p=128)[:, :, m * 128:(m + 1) * 128],
                  in_=wst[bi].rearrange("p (k c) -> p k c", c=128),
                  incs=[qx.inc(16)])
                wst_guard[bi] = qx.now()

            for t in range(16):
                bi = t % 2
                if xt_guard[0] is not None:
                    W("sync", xt_guard[0])
                E("sync", "dma_start", out=xt[bi][:],
                  in_=x_d[t * 128:(t + 1) * 128, :], incs=[qx.inc(16)])
                xld = qx.n
                W("vector", qx, xld)
                W("vector", dv)
                for i8 in range(8):
                    E("vector", "bn_stats", bn48[:, i8 * 6:(i8 + 1) * 6],
                      xt[bi][:, i8 * 512:(i8 + 1) * 512],
                      incs=([dv.inc()] if i8 == 7 else []))
                W("vector", dv)
                E("vector", "bn_aggr", bnag[:], bn48[:], incs=[dv.inc()])
                W("scalar", dv)
                E("scalar", "activation", sdv[:], bnag[:, 1:2], AF.Sqrt,
                  bias=cbias[:, 0:1], scale=1.0, incs=[ev.inc()])
                W("vector", ev)
                E("vector", "reciprocal", rstd[:], sdv[:], incs=[dv.inc()])
                W("vector", dv)
                E("vector", "tensor_scalar", nbias[:], bnag[:, 0:1],
                  rstd[:], -1.0, OP.mult, OP.mult, incs=[dv.inc()])
                if ln1_guard[0] is not None:
                    W("vector", ln1_guard[0])
                W("vector", dv)
                E("vector", "tensor_scalar", ln1[bi][:], xt[bi][:],
                  rstd[:, 0:1], nbias[:, 0:1], OP.mult, OP.add,
                  incs=[dv.inc()])
                ln_done = dv.n
                xt_guard[0] = dv.now()
                for k in range(32):
                    tbi = k % 2
                    if tbank_guard[tbi] is not None:
                        W("tensor", tbank_guard[tbi])
                    if k == 0:
                        W("tensor", dv, ln_done)
                    E("tensor", "matmul", psT[tbi][:, 0:128],
                      ln1[bi][:, k * 128:(k + 1) * 128], ident[:],
                      start=True, stop=True, incs=[mm.inc()])
                    eng, ctr, op = (("scalar", ev, "copy") if k % 2 == 0
                                    else ("vector", dv, "tensor_copy"))
                    W(eng, mm)
                    E(eng, op,
                      lnT[:, k * TB + t * 128: k * TB + t * 128 + 128],
                      psT[tbi][:, 0:128], incs=[ctr.inc()])
                    tbank_guard[tbi] = ctr.now()
                ln1_guard[0] = mm.now()
                emit_wf_unit(2 * t)
                emit_wf_unit(2 * t + 1)
            lnT_ev = ev.n
            lnT_dv = dv.n

            units = []
            for m in range(32):
                units.append(("k", m))
            for m in range(32):
                units.append(("v", m))
            for m in range(32):
                units.append(("q", m))
            srcs = {"k": wk_d, "v": wv_d, "q": wq_d}
            outs = {"k": kTs_d, "v": vTs_d, "q": qTs_d}
            biases = {"k": bkc, "v": bvc, "q": bqe}
            bank_guard[0] = (ev.h, lnT_ev)
            bank_guard[1] = (dv.h, lnT_dv)
            v_done = {}
            for ui, (jn, m) in enumerate(units):
                bi = uig[0] % 2
                uig[0] += 1
                bs = (ui % 2) * 4
                if nat_guard[bi] is not None:
                    W("gpsimd", nat_guard[bi])
                E("gpsimd", "dma_start", out=nat[bi][:],
                  in_=srcs[jn][m * 128:(m + 1) * 128, :], incs=[qg.inc(16)])
                ld_here = qg.n
                if wst_guard[bi] is not None:
                    W("scalar", wst_guard[bi])
                    W("vector", wst_guard[bi])
                W("tensor", qg, ld_here)
                rhs = (G_bf[:, (m // 2) * 128:(m // 2 + 1) * 128]
                       if jn == "q" else ident[:])
                for k in range(32):
                    bk_i = bs + (k % 4)
                    sl = psP[bk_i]
                    if bank_guard[bk_i] is not None:
                        W("tensor", bank_guard[bk_i])
                    E("tensor", "matmul", sl[:, 0:128],
                      nat[bi][:, k * 128:(k + 1) * 128], rhs,
                      start=True, stop=True, incs=[mm.inc()])
                    eng, ctr, op = (("scalar", ev, "copy") if k % 2 == 0
                                    else ("vector", dv, "tensor_copy"))
                    W(eng, mm)
                    E(eng, op, wst[bi][:, k * 128:(k + 1) * 128],
                      sl[:, 0:128], incs=[ctr.inc()])
                    bank_guard[bk_i] = ctr.now()
                nat_guard[bi] = mm.now()
                tr_ev, tr_dv = ev.n, dv.n
                W("tensor", ev, tr_ev)
                W("tensor", dv, tr_dv)
                if jn == "k" and m == 0:
                    W("tensor", ev, lnT_ev)
                    W("tensor", dv, lnT_dv)
                for tt in range(4):
                    if bank_guard[bs + tt] is not None:
                        W("tensor", bank_guard[bs + tt])
                for k in range(32):
                    for tt in range(4):
                        E("tensor", "matmul", psP[bs + tt][:],
                          wst[bi][:, k * 128:(k + 1) * 128],
                          lnT[:, k * TB + tt * 512: k * TB + tt * 512 + 512],
                          start=(k == 0), stop=(k == 31),
                          incs=([mm.inc()] if (k == 31 and tt == 3) else []))
                grp_done = mm.n
                for tt in range(4):
                    si = tt
                    if pstage_guard[si] is not None:
                        W("scalar", pstage_guard[si])
                    W("scalar", mm, grp_done)
                    E("scalar", "activation", pstage[si][:], psP[bs + tt][:],
                      AF.Identity, bias=biases[jn][:, m:m + 1], scale=1.0,
                      incs=[ev.inc()])
                    bank_guard[bs + tt] = ev.now()
                    W("sync", ev)
                    E("sync", "dma_start",
                      out=outs[jn][m * 128:(m + 1) * 128, tt * 512:(tt + 1) * 512],
                      in_=pstage[si][:], incs=[qx.inc(16)])
                    pstage_guard[si] = qx.now()
                wst_guard[bi] = mm.now()
                if jn == "v":
                    v_done[m] = qx.n
                    if m % 2 == 1:
                        h = m // 2
                        W("gpsimd", qx, v_done[m])
                        E("gpsimd", "collective_compute", "AllGather", OP.bypass,
                          replica_groups=RG,
                          ins=[kTs_d[2 * h * 128:(2 * h + 2) * 128, :].opt()],
                          outs=[cok[h].opt()], incs=[cc.inc()])
                        E("gpsimd", "collective_compute", "AllGather", OP.bypass,
                          replica_groups=RG,
                          ins=[vTs_d[2 * h * 128:(2 * h + 2) * 128, :].opt()],
                          outs=[cov[h].opt()], incs=[cc.inc()])
            phase_barrier()
            p.flush()

            if stop_after == "P1":
                finish()
                return nc

        with ExitStack() as ph:
            PE_ = ph.enter_context
            q_sb = [PE_(nc.sbuf_tensor(f"qsb{i}", [128, 2 * TB], BF16)) for i in range(2)]
            k_sb = [PE_(nc.sbuf_tensor(f"ksb{i}", [128, 2 * L], BF16)) for i in range(2)]
            vT_sb = [PE_(nc.sbuf_tensor(f"vTsb{i}", [128, 2 * L], BF16)) for i in range(2)]
            v_sb = [PE_(nc.sbuf_tensor(f"vsb{i}", [128, 32 * 257], BF16)) for i in range(2)]
            P_sb = [PE_(nc.sbuf_tensor(f"Psb{i}", [128, 32 * 512], BF16)) for i in range(2)]
            astg = [PE_(nc.sbuf_tensor(f"astg{i}", [128, 4 * 256], BF16)) for i in range(2)]
            rcol = [PE_(nc.sbuf_tensor(f"rcol{i}", [128, 1], F32)) for i in range(4)]
            psL = [PE_(nc.psum_tensor(f"psL{i}", [128, 512], F32)) for i in range(4)]
            psV = [PE_(nc.psum_tensor(f"psV{i}", [128, 512], F32)) for i in range(2)]
            psO = [PE_(nc.psum_tensor(f"psO{i}", [128, 257], F32)) for i in range(2)]

            for b_ in range(2):
                E("vector", "memset",
                  v_sb[b_].rearrange("p (kc c) -> p kc c", c=257)[:, :, 256:257],
                  1.0, incs=[dv.inc()])
            ones_done = dv.n

            qkv_guard = [None, None]
            P_guard = [None, None]
            psL_guard = [None] * 4
            psV_guard = [None] * 2
            psO_guard = [None] * 2
            astg_guard = [None, None]

            def load_head(h):
                bi = h % 2
                if qkv_guard[bi] is not None:
                    W("sync", qkv_guard[bi])
                W("sync", cc, 2 * (h + 1))
                E("sync", "dma_start",
                  out=q_sb[bi].rearrange("p (j t) -> p j t", j=2),
                  in_=qTs_d.rearrange("(m p) t -> p m t", p=128)[:, 2 * h:2 * h + 2, :],
                  incs=[qx.inc(16)])
                for r_ in range(2):
                    E("sync", "dma_start",
                      out=k_sb[bi].rearrange("p (c r t) -> p c r t", c=2, r=2)[:, :, r_, :],
                      in_=cok[h].rearrange("(r c p) t -> p r c t", c=2, p=128)[:, r_, :, :],
                      incs=[qx.inc(16)])
                    E("sync", "dma_start",
                      out=vT_sb[bi].rearrange("p (c r t) -> p c r t", c=2, r=2)[:, :, r_, :],
                      in_=cov[h].rearrange("(r c p) t -> p r c t", c=2, p=128)[:, r_, :, :],
                      incs=[qx.inc(16)])
                return qx.n

            h_loaded = [None] * (NH + 1)
            h_loaded[0] = load_head(0)

            for h in range(NH):
                bi = h % 2
                if h + 1 < NH:
                    h_loaded[h + 1] = load_head(h + 1)
                W("tensor", qx, h_loaded[h])
                if h == 0:
                    W("tensor", dv, ones_done)
                for j in range(2):
                    for kc in range(32):
                        tbi = kc % 2
                        if psV_guard[tbi] is not None:
                            W("tensor", psV_guard[tbi])
                        E("tensor", "matmul", psV[tbi][:, 0:128],
                          vT_sb[bi][:, j * L + kc * 128: j * L + kc * 128 + 128],
                          ident[:], start=True, stop=True, incs=[mm.inc()])
                        eng, ctr, op = (("scalar", ev, "copy") if kc % 2 == 0
                                        else ("vector", dv, "tensor_copy"))
                        W(eng, mm)
                        E(eng, op,
                          v_sb[bi][:, kc * 257 + j * 128: kc * 257 + j * 128 + 128],
                          psV[tbi][:, 0:128], incs=[ctr.inc()])
                        psV_guard[tbi] = ctr.now()
                vsb_ev = ev.n
                vsb_dv = dv.n

                for pt in range(2):
                    qts = [2 * pt, 2 * pt + 1]
                    for kc in range(32):
                        lbs = [0 * 2 + (kc % 2), 1 * 2 + (kc % 2)]
                        for q2 in range(2):
                            if psL_guard[lbs[q2]] is not None:
                                W("tensor", psL_guard[lbs[q2]])
                        for j in range(2):
                            for q2 in range(2):
                                E("tensor", "matmul", psL[lbs[q2]][:],
                                  k_sb[bi][:, j * L + kc * 128: j * L + kc * 128 + 128],
                                  q_sb[bi][:, j * TB + qts[q2] * 512: j * TB + qts[q2] * 512 + 512],
                                  start=(j == 0), stop=(j == 1),
                                  incs=([mm.inc()] if (j == 1 and q2 == 1) else []))
                        W("scalar", mm)
                        for q2 in range(2):
                            if kc == 0 and P_guard[q2] is not None:
                                W("scalar", P_guard[q2])
                            E("scalar", "activation",
                              P_sb[q2][:, kc * 512:(kc + 1) * 512],
                              psL[lbs[q2]][:],
                              AF.Exp, bias=cbias[:, 1:2], scale=1.0,
                              incs=[ev.inc()])
                            psL_guard[lbs[q2]] = ev.now()
                    exp_done = ev.n
                    W("tensor", ev, exp_done)
                    W("tensor", dv, vsb_dv)
                    W("tensor", ev, vsb_ev)
                    for q2 in range(2):
                        qt = qts[q2]
                        for qc in range(4):
                            ob = qc % 2
                            if psO_guard[ob] is not None:
                                W("tensor", psO_guard[ob])
                            for kc in range(32):
                                E("tensor", "matmul", psO[ob][:],
                                  P_sb[q2][:, kc * 512 + qc * 128: kc * 512 + qc * 128 + 128],
                                  v_sb[bi][:, kc * 257: kc * 257 + 257],
                                  start=(kc == 0), stop=(kc == 31),
                                  incs=([mm.inc()] if kc == 31 else []))
                            W("vector", mm)
                            W("vector", dv)
                            E("vector", "reciprocal", rcol[qc][:], psO[ob][:, 256:257],
                              incs=[dv.inc()])
                            W("vector", dv)
                            si = qt % 2
                            if qc == 0 and astg_guard[si] is not None:
                                W("vector", astg_guard[si])
                            E("vector", "tensor_scalar",
                              astg[si][:, qc * 256:(qc + 1) * 256],
                              psO[ob][:, 0:256], rcol[qc][:], None, OP.mult,
                              incs=[dv.inc()])
                            psO_guard[ob] = dv.now()
                        P_guard[q2] = mm.now()
                        W("sync", dv)
                        E("sync", "dma_start",
                          out=attns_d[qt * 512:(qt + 1) * 512,
                                      h * 256:(h + 1) * 256].rearrange("(g p) c -> p g c", p=128),
                          in_=astg[qt % 2].rearrange("p (g c) -> p g c", c=256),
                          incs=[qx.inc(16)])
                        astg_guard[qt % 2] = qx.now()
                qkv_guard[bi] = mm.now()
            phase_barrier()
            p.flush()

            if stop_after == "P2":
                finish()
                return nc

        with ExitStack() as ph:
            PE_ = ph.enter_context
            SBT = 1024
            XinT = PE_(nc.sbuf_tensor("XinT", [128, 32 * SBT], BF16))
            x1 = [PE_(nc.sbuf_tensor(f"x1_{i}", [128, D], F32)) for i in range(2)]
            ln2 = [PE_(nc.sbuf_tensor(f"ln2_{i}", [128, D], BF16)) for i in range(2)]
            bn48b = PE_(nc.sbuf_tensor("bn48b", [128, 48], F32))
            bnagb = PE_(nc.sbuf_tensor("bnagb", [128, 2], F32))
            sdvb = PE_(nc.sbuf_tensor("sdvb", [128, 1], F32))
            rstdb = PE_(nc.sbuf_tensor("rstdb", [128, 1], F32))
            nbiasb = PE_(nc.sbuf_tensor("nbiasb", [128, 1], F32))
            biasb = PE_(nc.sbuf_tensor("biasb", [128, D], BF16))
            bfrow = PE_(nc.sbuf_tensor("bfrow", [1, 1024], F32))
            gwbf = PE_(nc.sbuf_tensor("gwbf", [8, D], BF16))
            gwT = PE_(nc.sbuf_tensor("gwT", [128, 32 * 8], BF16))
            icsbf = PE_(nc.sbuf_tensor("icsbf", [8, D], BF16))
            ocsbf = PE_(nc.sbuf_tensor("ocsbf", [8, D], BF16))
            rT = PE_(nc.sbuf_tensor("rT", [8, TB], BF16))
            nmax8 = PE_(nc.sbuf_tensor("nmax8", [128, 1], F32))
            er8 = PE_(nc.sbuf_tensor("er8", [128, 8], F32))
            rs8 = PE_(nc.sbuf_tensor("rs8", [128, 1], F32))
            rr8 = PE_(nc.sbuf_tensor("rr8", [128, 1], F32))
            rn8 = PE_(nc.sbuf_tensor("rn8", [128, 8], BF16))
            wf_all = [PE_(nc.sbuf_tensor(f"wfall{i}", [128, 32 * 256], BF16)) for i in range(2)]
            os_all = PE_(nc.sbuf_tensor("os_all", [128, 8 * 256], F32))
            x1b_sb = [PE_(nc.sbuf_tensor(f"x1b{i}", [128, 256], F32)) for i in range(2)]
            t1_sb = [PE_(nc.sbuf_tensor(f"t1s{i}", [128, 256], F32)) for i in range(2)]
            outf = [PE_(nc.sbuf_tensor(f"outf{i}", [128, 256], F32)) for i in range(2)]
            psA = [PE_(nc.psum_tensor(f"psA{i}", [128, 256], F32)) for i in range(4)]
            psB = [PE_(nc.psum_tensor(f"psB{i}", [128, 256], F32)) for i in range(4)]

            E("gpsimd", "dma_start", out=gwbf[:], in_=gw_d, incs=[qg.inc(16)])
            E("gpsimd", "dma_start", out=icsbf[:], in_=ics_d, incs=[qg.inc(16)])
            E("gpsimd", "dma_start", out=ocsbf[:], in_=ocs_d, incs=[qg.inc(16)])
            prep_qg = qg.n
            bb_bank = [None] * 4
            for part in range(4):
                W("sync", mm)
                E("sync", "dma_start", out=bfrow[:],
                  in_=bf_d[:, part * 1024:(part + 1) * 1024], incs=[qx.inc(16)])
                W("tensor", qx, qx.n)
                for i4 in range(4):
                    i = part * 4 + i4
                    pb = psA[i % 4]
                    if bb_bank[i % 4] is not None:
                        W("tensor", bb_bank[i % 4])
                    E("tensor", "matmul", pb[:, 0:256], ones1f[:],
                      bfrow[:, i4 * 256:(i4 + 1) * 256], start=True, stop=True,
                      incs=[mm.inc()])
                    W("vector", mm)
                    E("vector", "tensor_copy", biasb[:, i * 256:(i + 1) * 256],
                      pb[:, 0:256], incs=[dv.inc()])
                    bb_bank[i % 4] = dv.now()
            W("tensor", qg, prep_qg)
            gw_bank = [None, None]
            for k in range(32):
                pb = psB[k % 2]
                if gw_bank[k % 2] is not None:
                    W("tensor", gw_bank[k % 2])
                E("tensor", "matmul", pb[:, 0:8], gwbf[:, k * 128:(k + 1) * 128],
                  ident8[:], start=True, stop=True, incs=[mm.inc()])
                W("scalar", mm)
                E("scalar", "copy", gwT[:, k * 8:(k + 1) * 8], pb[:, 0:8],
                  incs=[ev.inc()])
                gw_bank[k % 2] = ev.now()
            gw_done = ev.n

            x1_guard = [None, None]
            ln2_guard = [None, None]
            t3bank = [bb_bank[0], bb_bank[1]]
            isbank = [gw_bank[0], gw_bank[1]]
            rbank = bb_bank[2]
            rtbank = bb_bank[3]
            wfall_guard = [None, None]
            osb_guard = None
            mainbank = [None] * 7
            x1b_guard = [None, None]
            t1_guard = [None, None]
            outf_guard = [None, None]
            osa_guard = [None] * 8

            for sb in range(2):
                for g_ in mainbank + [osb_guard]:
                    if g_ is not None:
                        W("tensor", g_)
                for t in range(8):
                    tt = sb * 8 + t
                    bi = t % 2
                    r0 = tt * 128
                    if x1_guard[bi] is not None:
                        W("sync", x1_guard[bi])
                    E("sync", "dma_start", out=x1[bi][:],
                      in_=x_d[r0:r0 + 128, :], incs=[qx.inc(16)])
                    if ln2_guard[bi] is not None:
                        W("sync", ln2_guard[bi])
                    E("sync", "dma_start", out=ln2[bi][:],
                      in_=attns_d[r0:r0 + 128, :], incs=[qx.inc(16)])
                    t_ld = qx.n
                    W("vector", qx, t_ld)
                    W("vector", dv)
                    E("vector", "tensor_add", x1[bi][:], x1[bi][:], ln2[bi][:],
                      incs=[dv.inc()])
                    W("vector", dv)
                    for i8 in range(8):
                        E("vector", "bn_stats", bn48b[:, i8 * 6:(i8 + 1) * 6],
                          x1[bi][:, i8 * 512:(i8 + 1) * 512],
                          incs=([dv.inc()] if i8 == 7 else []))
                    W("vector", dv)
                    E("vector", "bn_aggr", bnagb[:], bn48b[:], incs=[dv.inc()])
                    W("scalar", dv)
                    E("scalar", "activation", sdvb[:], bnagb[:, 1:2], AF.Sqrt,
                      bias=cbias[:, 0:1], scale=1.0, incs=[ev.inc()])
                    W("vector", ev)
                    E("vector", "reciprocal", rstdb[:], sdvb[:], incs=[dv.inc()])
                    W("vector", dv)
                    E("vector", "tensor_scalar", nbiasb[:], bnagb[:, 0:1],
                      rstdb[:], -1.0, OP.mult, OP.mult, incs=[dv.inc()])
                    if ln2_guard[bi] is not None:
                        W("scalar", ln2_guard[bi])
                    W("scalar", dv)
                    E("scalar", "activation", ln2[bi][:], x1[bi][:], AF.Identity,
                      bias=nbiasb[:, 0:1], scale=rstdb[:, 0:1], incs=[ev.inc()])
                    ln2_done = ev.n
                    W("vector", ev, ln2_done)
                    W("vector", dv)
                    E("vector", "tensor_add", x1[bi][:], x1[bi][:], biasb[:],
                      incs=[dv.inc()])
                    W("sync", dv)
                    E("sync", "dma_start", out=x1s_d[tt * 128:(tt + 1) * 128, :],
                      in_=x1[bi][:], incs=[qx.inc(16)])
                    x1_guard[bi] = qx.now()
                    for k in range(32):
                        tbi = k % 2
                        if t3bank[tbi] is not None:
                            W("tensor", t3bank[tbi])
                        if k == 0:
                            W("tensor", ev, ln2_done)
                        E("tensor", "matmul", psA[tbi][:, 0:128],
                          ln2[bi][:, k * 128:(k + 1) * 128], ident[:],
                          start=True, stop=True, incs=[mm.inc()])
                        eng, ctr, op = (("scalar", ev, "copy") if k % 2 == 0
                                        else ("vector", dv, "tensor_copy"))
                        W(eng, mm)
                        E(eng, op,
                          XinT[:, k * SBT + t * 128: k * SBT + t * 128 + 128],
                          psA[tbi][:, 0:128], incs=[ctr.inc()])
                        t3bank[tbi] = ctr.now()
                    ln2_guard[bi] = mm.now()
                    xinT_ev, xinT_dv = ev.n, dv.n
                    if rbank is not None:
                        W("tensor", rbank)
                    W("tensor", ev, xinT_ev)
                    W("tensor", dv, xinT_dv)
                    for k in range(32):
                        E("tensor", "matmul", psA[2][:, 0:8],
                          XinT[:, k * SBT + t * 128: k * SBT + t * 128 + 128],
                          gwT[:, k * 8:(k + 1) * 8],
                          start=(k == 0), stop=(k == 31),
                          incs=([mm.inc()] if k == 31 else []))
                    router_done = mm.n
                    W("vector", mm, router_done)
                    W("vector", dv)
                    E("vector", "reduce_max", nmax8[:], psA[2][:, 0:8], AX,
                      negate=True, incs=[dv.inc()])
                    W("scalar", dv)
                    W("scalar", ev, gw_done)
                    E("scalar", "activation", er8[:], psA[2][:, 0:8], AF.Exp,
                      bias=nmax8[:, 0:1], scale=1.0, incs=[ev.inc()])
                    rbank = ev.now()
                    W("vector", ev)
                    E("vector", "reduce_sum", rs8[:], er8[:], AX, incs=[dv.inc()])
                    W("vector", dv)
                    E("vector", "reciprocal", rr8[:], rs8[:], incs=[dv.inc()])
                    W("vector", dv)
                    E("vector", "tensor_scalar", rn8[:], er8[:], rr8[:], None,
                      OP.mult, incs=[dv.inc()])
                    rn_done = dv.n
                    if rtbank is not None:
                        W("tensor", rtbank)
                    W("tensor", dv, rn_done)
                    E("tensor", "matmul", psA[3][0:8, 0:128], rn8[:], ident[:],
                      start=True, stop=True, incs=[mm.inc()])
                    W("scalar", mm)
                    E("scalar", "copy", rT[:, tt * 128:(tt + 1) * 128],
                      psA[3][0:8, 0:128], incs=[ev.inc()])
                    rtbank = ev.now()
                    rT_done = ev.n
                    W("tensor", ev, rT_done)
                    for k in range(32):
                        ib = k % 2
                        if isbank[ib] is not None:
                            W("tensor", isbank[ib])
                        E("tensor", "matmul", psB[ib][:, 0:128],
                          icsbf[:, k * 128:(k + 1) * 128],
                          rT[:, tt * 128:(tt + 1) * 128],
                          start=True, stop=True, incs=[mm.inc()])
                        W("vector", mm)
                        E("vector", "tensor_mul",
                          XinT[:, k * SBT + t * 128: k * SBT + t * 128 + 128],
                          XinT[:, k * SBT + t * 128: k * SBT + t * 128 + 128],
                          psB[ib][:, 0:128], incs=[dv.inc()])
                        isbank[ib] = dv.now()
                xin_ready_dv = dv.n
                xin_ready_ev = ev.n

                W("tensor", dv, xin_ready_dv)
                W("tensor", ev, xin_ready_ev)
                banks7 = psA + psB[:3]
                for oc in range(16):
                    gc = sb * 16 + oc
                    wb = gc % 2
                    if wfall_guard[wb] is not None:
                        W("gpsimd", wfall_guard[wb])
                    for q4 in range(4):
                        E("gpsimd", "dma_start",
                          out=wf_all[wb].rearrange("p (k r) -> p k r", r=256)[
                              :, q4 * 8:(q4 + 1) * 8, :],
                          in_=wfT_d.rearrange("(k p) r -> p k r", p=128)[
                              :, q4 * 8:(q4 + 1) * 8, oc * 256:(oc + 1) * 256],
                          incs=[qg.inc(16)])
                    wf_ld = qg.n
                    W("scalar", qg, wf_ld)
                    if wfall_guard[wb] is not None:
                        W("scalar", wfall_guard[wb])
                    E("scalar", "sign", wf_all[wb][:], wf_all[wb][:],
                      incs=[ev.inc()])
                    wf_sgn = ev.n
                    W("tensor", ev, wf_sgn)
                    for tc in range(8):
                        tg = sb * 8 + tc
                        if osb_guard is not None:
                            W("tensor", osb_guard)
                        E("tensor", "matmul", psB[3][:, 0:256],
                          rT[:, tg * 128:(tg + 1) * 128],
                          ocsbf[:, oc * 256:(oc + 1) * 256],
                          start=True, stop=True, incs=[mm.inc()])
                        W("scalar", mm)
                        if osa_guard[tc] is not None:
                            W("scalar", osa_guard[tc])
                        E("scalar", "copy", os_all[:, tc * 256:(tc + 1) * 256],
                          psB[3][:, 0:256], incs=[ev.inc()])
                        osb_guard = ev.now()
                        os_ev = ev.n
                        bidx = (gc * 8 + tc) % 7
                        bank = banks7[bidx]
                        if mainbank[bidx] is not None:
                            W("tensor", mainbank[bidx])
                        for k in range(32):
                            E("tensor", "matmul", bank[:, 0:256],
                              XinT[:, k * SBT + tc * 128: k * SBT + tc * 128 + 128],
                              wf_all[wb][:, k * 256:(k + 1) * 256],
                              start=(k == 0), stop=(k == 31),
                              incs=([mm.inc()] if k == 31 else []))
                        main_done = mm.n
                        eb = tc % 2
                        if x1b_guard[eb] is not None:
                            W("sync", x1b_guard[eb])
                        E("sync", "dma_start", out=x1b_sb[eb][:],
                          in_=x1s_d[tg * 128:(tg + 1) * 128,
                                    oc * 256:(oc + 1) * 256],
                          incs=[qx.inc(16)])
                        eld = qx.n
                        W("vector", mm, main_done)
                        W("vector", qx, eld)
                        W("vector", ev, os_ev)
                        if t1_guard[eb] is not None:
                            W("vector", t1_guard[eb])
                        E("vector", "tensor_mul", t1_sb[eb][:],
                          bank[:, 0:256], os_all[:, tc * 256:(tc + 1) * 256],
                          incs=[dv.inc()])
                        osa_guard[tc] = dv.now()
                        mainbank[bidx] = dv.now()
                        W("vector", dv)
                        if outf_guard[eb] is not None:
                            W("vector", outf_guard[eb])
                        E("vector", "tensor_add", outf[eb][:], t1_sb[eb][:],
                          x1b_sb[eb][:], incs=[dv.inc()])
                        t1_guard[eb] = dv.now()
                        x1b_guard[eb] = dv.now()
                        W("sync", dv)
                        E("sync", "dma_start",
                          out=out_d[tg * 128:(tg + 1) * 128,
                                    oc * 256:(oc + 1) * 256],
                          in_=outf[eb][:], incs=[qx.inc(16)])
                        outf_guard[eb] = qx.now()
                    wfall_guard[wb] = mm.now()
            if debug_scratch:
                dbg["rT"] = (nc.dram_tensor("d_rT", [8, TB], BF16,
                                            kind="ExternalOutput").ap(), rT[:])
                dbg["XinT"] = (nc.dram_tensor("d_XinT", [128, 32 * SBT], BF16,
                                              kind="ExternalOutput").ap(), XinT[:])
                dbg["osall"] = (nc.dram_tensor("d_osall", [128, 8 * 256], F32,
                                               kind="ExternalOutput").ap(), os_all[:])
            finish()
        return nc


def _layout_inputs(inp, core):
    b = core // 2
    hg = core % 2
    ts = slice(hg * TB, (hg + 1) * TB)
    m = {
        "x": np.ascontiguousarray(np.asarray(inp["L0_attn_x"][b][ts])),
        "wq": np.ascontiguousarray(np.asarray(inp["L0_attn_Wq"])),
        "wk": np.ascontiguousarray(np.asarray(inp["L0_attn_Wk"])),
        "wv": np.ascontiguousarray(np.asarray(inp["L0_attn_Wv"])),
        "bq": np.ascontiguousarray(np.asarray(inp["L0_attn_bq"]).reshape(32, 128).T),
        "bk": np.ascontiguousarray(np.asarray(inp["L0_attn_bk"]).reshape(32, 128).T),
        "bv": np.ascontiguousarray(np.asarray(inp["L0_attn_bv"]).reshape(32, 128).T),
        "wf": np.ascontiguousarray(np.asarray(inp["L0_ffn_weight"])),
        "bfb": np.ascontiguousarray(np.asarray(inp["L0_ffn_bias"]).reshape(1, D)),
        "gw": np.ascontiguousarray(np.asarray(inp["L0_ffn_gate_w"])),
        "ics": np.ascontiguousarray(np.asarray(inp["L0_ffn_in_channel_scale"])),
        "ocs": np.ascontiguousarray(np.asarray(inp["L0_ffn_out_channel_scale"])),
        "bwf": np.ascontiguousarray(np.asarray(inp["L0_attn_bias_weight"]).reshape(1, 48)),
    }
    return m


def kernel(**inputs):
    nc = build()
    in_maps = [_layout_inputs(inputs, c) for c in range(8)]
    res = run_bass_kernel_spmd(nc, in_maps, core_ids=list(range(8)))
    out = np.empty((4, L, D), dtype=np.float32)
    for c in range(8):
        b, hg = c // 2, c % 2
        out[b, hg * TB:(hg + 1) * TB, :] = res.results[c]["out"]
    return out
